# version 03 (final)
# baseline (speedup 1.0000x reference)
import contextlib

import numpy as np
import ml_dtypes

import concourse.bass as bass
import concourse.mybir as mybir
from concourse.bass_utils import run_bass_kernel_spmd

BF = ml_dtypes.bfloat16
F32 = mybir.dt.float32
BF16 = mybir.dt.bfloat16
I32 = mybir.dt.int32
AF = mybir.ActivationFunctionType
ALU = mybir.AluOpType
AX = mybir.AxisListType
f32i = np.float32

D = 2048
S = 2048
NT = 16
NKC = 16
MLP = 16384
NCORES = 8
_STAGE = 6
_DBG = False
_A_GROUPS = 8
_A_ATTN = True
_A_OUT = True
_A_LVL = 5
_DECODE = True
INV_SQRT_D = 1.0 / np.sqrt(np.float32(2048.0))



def _host_prep(inp):
    out = {"shared": {}, "per_core": [dict() for _ in range(NCORES)]}
    sh = out["shared"]

    x = np.asarray(inp["L0_attn_a_x"], dtype=np.float32)
    for c in range(NCORES):
        out["per_core"][c]["x_in"] = np.ascontiguousarray(x[c])

    WQ = np.asarray(inp["L0_attn_a_W_Q"], dtype=np.float32)
    WK = np.asarray(inp["L0_attn_a_W_K"], dtype=np.float32)
    WV = np.asarray(inp["L0_attn_a_W_V"], dtype=np.float32)
    WO = np.asarray(inp["L0_attn_a_W_O"], dtype=np.float32)
    sh["wqt"] = np.ascontiguousarray(WQ.reshape(128, 16, D).transpose(2, 1, 0).reshape(D, D))
    sh["wkt"] = np.ascontiguousarray(WK.reshape(128, 8, D).transpose(2, 1, 0).reshape(D, 1024))
    sh["wvt"] = np.ascontiguousarray(WV.reshape(128, 8, D).transpose(2, 1, 0).reshape(D, 1024))
    sh["wot"] = np.ascontiguousarray(WO.T)

    sh["wqkvt"] = np.ascontiguousarray(np.asarray(inp["L0_attn_b_w_qkv"], dtype=np.float32).T)
    sh["wgatet"] = np.ascontiguousarray(np.asarray(inp["L0_attn_b_w_gate"], dtype=np.float32).T)
    sh["woutt"] = np.ascontiguousarray(np.asarray(inp["L0_attn_b_w_out"], dtype=np.float32).T)
    sh["nw16"] = np.ascontiguousarray(
        np.asarray(inp["L0_attn_b_norm_w"], dtype=np.float32).reshape(16, 128).T)
    sh["slopes"] = np.ascontiguousarray(
        np.asarray(inp["L0_attn_b_slope_rate"], dtype=np.float32).reshape(1, 16))

    GpT = np.ascontiguousarray(np.asarray(inp["L0_ffn_gate_W_packed"], dtype=np.int32).T)
    UpT = np.ascontiguousarray(np.asarray(inp["L0_ffn_up_W_packed"], dtype=np.int32).T)
    DpT = np.ascontiguousarray(np.asarray(inp["L0_ffn_down_W_packed"], dtype=np.int32).T)
    scgT = np.ascontiguousarray(np.asarray(inp["L0_ffn_gate_scales"], dtype=np.float32).T)
    scuT = np.ascontiguousarray(np.asarray(inp["L0_ffn_up_scales"], dtype=np.float32).T)
    scdT = np.ascontiguousarray(np.asarray(inp["L0_ffn_down_scales"], dtype=np.float32).T)
    for c in range(NCORES):
        pc = out["per_core"][c]
        pc["gp_sh"] = np.ascontiguousarray(GpT[128 * c:128 * (c + 1)])
        pc["up_sh"] = np.ascontiguousarray(UpT[128 * c:128 * (c + 1)])
        pc["dp_sh"] = np.ascontiguousarray(DpT[1024 * c:1024 * (c + 1)])
        pc["scg_sh"] = np.ascontiguousarray(scgT[8 * c:8 * (c + 1)])
        pc["scu_sh"] = np.ascontiguousarray(scuT[8 * c:8 * (c + 1)])
        pc["scd_sh"] = np.ascontiguousarray(scdT[64 * c:64 * (c + 1)])

    sh["ident"] = np.eye(128, dtype=BF)
    half = 64
    inv_freq = (1.0 / 10000.0 ** (np.arange(0, half, dtype=np.float64) * 2.0 / 128.0))
    pos = np.arange(S, dtype=np.float64)
    ang = pos[:, None] * inv_freq[None, :]
    cosv = np.cos(ang).astype(np.float32)
    sinv = np.sin(ang).astype(np.float32)
    cosT = np.concatenate([cosv.T, cosv.T], axis=0)
    sinfT = np.concatenate([-sinv.T, sinv.T], axis=0)
    sh["cosT"] = cosT.astype(BF)
    sh["sinfT"] = sinfT.astype(BF)
    m = np.arange(256, dtype=np.float32)
    i0 = m[None, :] - np.arange(128, dtype=np.float32)[:, None]
    i1 = m[None, :] - (128 + np.arange(128, dtype=np.float32))[:, None]
    sh["idxT0"] = np.where(i0 >= 0, i0, 30000.0).astype(BF)
    sh["idxT1"] = np.where(i1 >= 0, i1, 30000.0).astype(BF)
    arr = np.arange(1, 257, dtype=np.float32)
    sh["arrq"] = np.broadcast_to(arr, (128, 256)).astype(BF).copy()
    sh["arrk"] = np.broadcast_to(256.0 - arr, (128, 256)).astype(BF).copy()
    sh["onesbf"] = np.ones((128, 1), dtype=BF)
    sh["onesf1"] = np.ones((1, 128), dtype=np.float32)
    selT = np.zeros((16, 2048), dtype=BF)
    for j in range(16):
        selT[j, j * 128:(j + 1) * 128] = 1
    sh["selT"] = selT
    sh["identf"] = np.eye(128, dtype=np.float32)
    return out



class KB:
    ENGS = ("sync", "tensor", "vector", "scalar", "gpsimd")

    def __init__(self):
        self.q = {e: [] for e in self.ENGS}
        self.sems = {}
        self.cnt = {}
        self.lastw = {e: {} for e in self.ENGS}

    def raw(self, eng, fn):
        self.q[eng].append(fn)

    def tok(self, sem, n=1):
        self.cnt[sem] = self.cnt.get(sem, 0) + n
        return (sem, self.cnt[sem])

    def cur(self, sem):
        return (sem, self.cnt.get(sem, 0))

    def wait(self, eng, *toks):
        for t in toks:
            if t is None:
                continue
            sem, v = t
            if v <= 0 or self.lastw[eng].get(sem, 0) >= v:
                continue
            self.lastw[eng][sem] = v
            self.raw(eng, lambda e, s=sem, vv=v: e.wait_ge(self.sems[s], vv))

    def do(self, eng, fn, sem=None, n=1):
        if sem is None:
            self.raw(eng, fn)
            return None
        t = self.tok(sem, n)
        self.raw(eng, lambda e, f=fn, s=sem, nn=n: f(e).then_inc(self.sems[s], nn))
        return t

    def dma(self, eng, out, in_, sem="w", **kw):
        return self.do(eng, (lambda e, o=out, i=in_, k=dict(kw): e.dma_start(out=o, in_=i, **k)), sem, 16)


class Arena:

    def __init__(self, t, n):
        self.t = t
        self.n = n
        self.off = 0

    def bf16(self, cols):
        assert self.off + cols <= self.n, f"arena overflow {self.off}+{cols}>{self.n}"
        ap = self.t[:, self.off:self.off + cols]
        self.off += cols
        return ap

    def f32(self, cols):
        if self.off % 2:
            self.off += 1
        ap = self.t[:, self.off:self.off + 2 * cols].bitcast(F32)
        self.off += 2 * cols
        return ap

    def i32(self, cols):
        if self.off % 2:
            self.off += 1
        ap = self.t[:, self.off:self.off + 2 * cols].bitcast(I32)
        self.off += 2 * cols
        return ap

    def reset(self, off=0):
        self.off = off
        return self



def _build():
    nc = bass.Bass("TRN2", target_bir_lowering=False)

    ap = lambda name, shape, dt: nc.dram_tensor(name, shape, dt, kind="ExternalInput").ap()
    x_in = ap("x_in", [S, D], F32)
    gp = ap("gp_sh", [128, MLP], I32)
    up = ap("up_sh", [128, MLP], I32)
    dp = ap("dp_sh", [1024, D], I32)
    scg = ap("scg_sh", [8, MLP], F32)
    scu = ap("scu_sh", [8, MLP], F32)
    scd = ap("scd_sh", [64, D], F32)
    wqt = ap("wqt", [D, D], F32)
    wkt = ap("wkt", [D, 1024], F32)
    wvt = ap("wvt", [D, 1024], F32)
    wot = ap("wot", [D, D], F32)
    wqkvt = ap("wqkvt", [D, 6144], F32)
    wgatet = ap("wgatet", [D, D], F32)
    woutt = ap("woutt", [D, D], F32)
    nw16 = ap("nw16", [128, 16], F32)
    slopes = ap("slopes", [1, 16], F32)
    ident_in = ap("ident", [128, 128], BF16)
    cos_in = ap("cosT", [128, S], BF16)
    sinf_in = ap("sinfT", [128, S], BF16)
    idx0_in = ap("idxT0", [128, 256], BF16)
    idx1_in = ap("idxT1", [128, 256], BF16)
    arrq_in = ap("arrq", [128, 256], BF16)
    arrk_in = ap("arrk", [128, 256], BF16)
    onesbf_in = ap("onesbf", [128, 1], BF16)
    onesf1_in = ap("onesf1", [1, 128], F32)
    selT_in = ap("selT", [16, 2048], BF16)
    identf_in = ap("identf", [128, 128], F32)
    y_out = nc.dram_tensor("y", [S, D], F32, kind="ExternalOutput").ap()
    if _DBG:
        dbg_w = nc.dram_tensor("dbg_w", [128, 4096], BF16, kind="ExternalOutput").ap()
        dbg_b = nc.dram_tensor("dbg_b", [128, 256], F32, kind="ExternalOutput").ap()
        dbg_q = nc.dram_tensor("dbg_q", [128, S], BF16, kind="ExternalOutput").ap()
        dbg_k = nc.dram_tensor("dbg_k", [128, S], BF16, kind="ExternalOutput").ap()
        dbg_c = nc.dram_tensor("dbg_c", [128, S], BF16, kind="ExternalOutput").ap()
        dbg_x = nc.dram_tensor("dbg_x", [2, 128, S], BF16, kind="ExternalOutput").ap()
        dbg_kp = nc.dram_tensor("dbg_kp", [128, S], BF16, kind="ExternalOutput").ap()
        dbg_xs = nc.dram_tensor("dbg_xs", [128, D], BF16, kind="ExternalOutput").ap()
        dbg_st = nc.dram_tensor("dbg_st", [128, 8], F32, kind="ExternalOutput").ap()
        dbg_x0 = nc.dram_tensor("dbg_x0", [128, D], F32, kind="ExternalOutput").ap()
        dbg_ts = nc.dram_tensor("dbg_ts", [128, D], BF16, kind="ExternalOutput").ap()

    cg = nc.dram_tensor("cg", [256, MLP], BF16).ap()
    cu = nc.dram_tensor("cu", [256, MLP], BF16).ap()
    cd = nc.dram_tensor("cd", [2048, D], BF16).ap()
    Gg = nc.dram_tensor("Gg", [2048, MLP], BF16).ap()
    Gu = nc.dram_tensor("Gu", [2048, MLP], BF16).ap()
    Gd = nc.dram_tensor("Gd", [MLP, D], BF16).ap()
    xlnsp = nc.dram_tensor("xlnsp", [NKC, 128, S], BF16).ap()
    xsp = nc.dram_tensor("xsp", [NT, 128, D], F32).ap()

    kb = KB()
    es = contextlib.ExitStack()
    with es:
        ident = es.enter_context(nc.sbuf_tensor("identt", [128, 128], BF16))
        idx0 = es.enter_context(nc.sbuf_tensor("idx0t", [128, 256], BF16))
        idx1 = es.enter_context(nc.sbuf_tensor("idx1t", [128, 256], BF16))
        arrq = es.enter_context(nc.sbuf_tensor("arrqt", [128, 256], BF16))
        arrk = es.enter_context(nc.sbuf_tensor("arrkt", [128, 256], BF16))
        onesbf = es.enter_context(nc.sbuf_tensor("onesbft", [128, 1], BF16))
        onesf1 = es.enter_context(nc.sbuf_tensor("onesf1t", [1, 128], F32))
        mred = es.enter_context(nc.sbuf_tensor("mred", [128, 4], F32))
        negmc = es.enter_context(nc.sbuf_tensor("negmc", [128, 16], F32))
        mT16 = es.enter_context(nc.sbuf_tensor("mT16", [16, 128], BF16))
        identf = es.enter_context(nc.sbuf_tensor("identft", [128, 128], F32))
        nw = es.enter_context(nc.sbuf_tensor("nwt", [128, 16], F32))
        snegb = es.enter_context(nc.sbuf_tensor("snegbt", [128, 16], F32))
        bdb = es.enter_context(nc.sbuf_tensor("bdbt", [128, 16], F32))
        xscr = es.enter_context(nc.sbuf_tensor("xscr", [128, D], BF16))
        sqscr = xscr
        st_sum = es.enter_context(nc.sbuf_tensor("st_sum", [128, 1], F32))
        st_sq = es.enter_context(nc.sbuf_tensor("st_sq", [128, 1], F32))
        st_mean = es.enter_context(nc.sbuf_tensor("st_mean", [128, 1], F32))
        st_msq = es.enter_context(nc.sbuf_tensor("st_msq", [128, 1], F32))
        st_var = es.enter_context(nc.sbuf_tensor("st_var", [128, 1], F32))
        st_ln = es.enter_context(nc.sbuf_tensor("st_ln", [128, 1], F32))
        st_rstd = es.enter_context(nc.sbuf_tensor("st_rstd", [128, 1], F32))
        bcs = es.enter_context(nc.sbuf_tensor("bcs", [128, 256], F32))
        rdenf = bcs
        xres_t = es.enter_context(nc.sbuf_tensor("xres", [128, NT * D], F32))
        a1_t = es.enter_context(nc.sbuf_tensor("a1", [128, 18432], BF16))
        a2_t = es.enter_context(nc.sbuf_tensor("a2", [128, 16384], BF16))
        ps = [es.enter_context(nc.psum_tensor(f"ps{i}", [128, 512], F32)) for i in range(8)]
        for s_ in ("w", "g", "p", "v", "a", "cc", "wx", "wc", "wt2", "wd1", "wd2",
                   "wsp", "wb0", "wb1", "wg0", "wg1", "wdt", "ws0", "ws1", "ws2", "ws3",
                   "g0", "g1", "go", "gw2", "wy", "wdbg"):
            kb.sems[s_] = es.enter_context(nc.semaphore("sem_" + s_))
        block = es.enter_context(nc.Block())

        XR = xres_t[:]
        A1 = Arena(a1_t[:], 18432)
        A2 = Arena(a2_t[:], 16384)
        psb = [ps[i][:].bitcast(BF16) for i in range(8)]

        t_x = kb.dma("sync", XR.rearrange("p (tt f) -> p tt f", tt=NT), x_in.rearrange("(tt p) f -> p tt f", p=128), sem="wx")
        for dst, src in ((ident[:], ident_in), (idx0[:], idx0_in), (idx1[:], idx1_in),
                         (arrq[:], arrq_in),
                         (arrk[:], arrk_in), (onesbf[:], onesbf_in), (onesf1[:], onesf1_in),
                         (identf[:], identf_in), (nw[:], nw16)):
            kb.dma("sync", dst, src, sem="wc")
        t_consts = kb.cur("wc")
        t_sl = kb.dma("sync", snegb[0:1, :], slopes, sem="wc")
        kb.wait("tensor", t_sl, t_consts)
        t_p = kb.do("tensor", lambda e: e.matmul(ps[0][:, 0:16], onesf1[:], snegb[0:1, :],
                                                 start=True, stop=True), "p")
        kb.wait("vector", t_p)
        tsn = kb.do("vector", lambda e: e.tensor_scalar(snegb[:], ps[0][:, 0:16], -1.0, None, op0=ALU.mult), "v")
        kb.wait("vector", tsn)
        kb.do("vector", lambda e: e.tensor_scalar(bdb[:], ps[0][:, 0:16], -256.0, None, op0=ALU.mult), "v")
        t_bd0 = kb.cur("v")
        kb.wait("scalar", t_bd0)
        t_bd = kb.do("scalar", lambda e: e.activation(bdb[:], bdb[:], AF.Exp), "a")
        t_smalls = (t_bd, kb.cur("v"))

        a = A1.reset()
        d_ct = a.i32(1024)
        d_sc = a.f32(1024)
        d_m2 = a.i32(1024)
        d_sb = a.i32(1024)
        d_ssc = a.f32(1024)
        d_mf = a.f32(1024)
        d_r1 = a.f32(1024)
        d_wt = a.bf16(1024)
        t_ct_free = None
        t_st_prev = None

        def decode_tile(psrc, scsrc, dst_lo, dst_hi):
            nonlocal t_ct_free, t_st_prev
            kb.wait("sync", t_ct_free)
            tl = kb.dma("sync", d_ct, psrc, sem="wd1")
            ts_ = kb.dma("sync", d_sc, scsrc, sem="wd1")
            for e, dst in ((0, dst_lo), (1, dst_hi)):
                kb.wait("vector", tl, ts_, t_st_prev)
                if e == 0:
                    kb.do("vector", lambda en: en.tensor_scalar(d_m2, d_ct, 7, 1, op0=ALU.bitwise_and, op1=ALU.logical_shift_left))
                    kb.do("vector", lambda en: en.tensor_scalar(d_sb, d_ct, 8, 28, op0=ALU.bitwise_and, op1=ALU.logical_shift_left))
                else:
                    kb.do("vector", lambda en: en.tensor_scalar(d_m2, d_ct, 0x70, 3, op0=ALU.bitwise_and, op1=ALU.logical_shift_right))
                    kb.do("vector", lambda en: en.tensor_scalar(d_sb, d_ct, 128, 24, op0=ALU.bitwise_and, op1=ALU.logical_shift_left), "v")
                if e == 1:
                    t_ct_free = kb.cur("v")
                kb.do("vector", lambda en: en.tensor_tensor(d_ssc.bitcast(I32), d_sc.bitcast(I32), d_sb, op=ALU.bitwise_xor))
                kb.do("vector", lambda en: en.tensor_copy(d_mf, d_m2))
                kb.do("vector", lambda en: en.tensor_scalar(d_r1, d_mf, -8.0, 0.0, op0=ALU.add, op1=ALU.max))
                kb.do("vector", lambda en: en.tensor_tensor(d_mf, d_mf, d_r1, op=ALU.add))
                kb.do("vector", lambda en: en.tensor_scalar(d_r1, d_mf, -16.0, 0.0, op0=ALU.add, op1=ALU.max))
                kb.do("vector", lambda en: en.tensor_tensor(d_mf, d_mf, d_r1, op=ALU.add))
                tv = kb.do("vector", lambda en: en.scalar_tensor_tensor(
                    d_wt, d_mf, 0.25, d_ssc, op0=ALU.mult, op1=ALU.mult), "v")
                kb.wait("sync", tv)
                t_st_prev = kb.dma("sync", dst, d_wt, sem="wd2")

        for wi in range(16 if _DECODE else 0):
            sl = slice(1024 * wi, 1024 * (wi + 1))
            decode_tile(gp[:, sl], scg[:, sl].unsqueeze(1).broadcast_to([8, 16, 1024]),
                        cg[0:128, sl], cg[128:256, sl])
        for wi in range(16 if _DECODE else 0):
            sl = slice(1024 * wi, 1024 * (wi + 1))
            decode_tile(up[:, sl], scu[:, sl].unsqueeze(1).broadcast_to([8, 16, 1024]),
                        cu[0:128, sl], cu[128:256, sl])
        for wt_i in range(8 if _DECODE else 0):
            for fw in range(2):
                sl = slice(1024 * fw, 1024 * (fw + 1))
                rs = slice(128 * wt_i, 128 * (wt_i + 1))
                decode_tile(dp[rs, sl],
                            scd[8 * wt_i:8 * wt_i + 8, sl].unsqueeze(1).broadcast_to([8, 16, 1024]),
                            cd[128 * wt_i:128 * wt_i + 128, sl],
                            cd[1024 + 128 * wt_i:1024 + 128 * wt_i + 128, sl])
        t_dec_stores = kb.cur("wd2")
        kb.wait("gpsimd", t_dec_stores)
        for cin, cout in (((cg, Gg), (cu, Gu), (cd, Gd)) if _DECODE else ()):
            kb.do("gpsimd", lambda e, ci=cin, co=cout: e.collective_compute(
                "AllGather", ALU.bypass, replica_groups=[list(range(NCORES))],
                ins=[ci.opt()], outs=[co.opt()]), "cc")
        t_gather = kb.cur("cc")

        bankfree = {i: None for i in range(8)}

        ALL_SEMS = ("p", "v", "a", "cc", "w", "g", "wx", "wc", "wt2", "wd1", "wd2",
                    "wsp", "wb0", "wb1", "wg0", "wg1", "wdt", "ws0", "ws1", "ws2", "ws3",
                    "g0", "g1", "go", "gw2", "wdbg")

        def barrier():
            toks = [kb.cur(s_) for s_ in ALL_SEMS]
            for eng in KB.ENGS:
                kb.wait(eng, *toks)

        def tile_home(tt):
            return XR[:, tt * D:(tt + 1) * D]

        t_xscr_free = [None]

        def vsync():
            t = kb.do("vector", lambda e: e.tensor_copy(st_sum[:], st_sum[:]), "v")
            kb.wait("vector", t)

        def ln_stats_apply(xtile, deps):
            kb.wait("scalar", *deps)
            kb.wait("scalar", t_xscr_free[0])
            ta = kb.do("scalar", lambda e: e.activation(sqscr[:], xtile, AF.Square, accum_out=st_sq[:]), "a")
            kb.wait("vector", *deps)
            t1 = kb.do("vector", lambda e: e.tensor_reduce(st_sum[:], xtile, AX.X, ALU.add), "v")
            kb.wait("vector", t1)
            t2 = kb.do("vector", lambda e: e.tensor_scalar(st_mean[:], st_sum[:], 1.0 / D, None, op0=ALU.mult), "v")
            kb.wait("vector", ta, t2)
            t3 = kb.do("vector", lambda e: e.tensor_scalar(st_msq[:], st_sq[:], 1.0 / D, None, op0=ALU.mult), "v")
            t4 = kb.do("vector", lambda e: e.tensor_tensor(st_var[:], st_mean[:], st_mean[:], op=ALU.mult), "v")
            kb.wait("vector", t3, t4)
            tv = kb.do("vector", lambda e: e.tensor_tensor(st_var[:], st_msq[:], st_var[:], op=ALU.subtract), "v")
            kb.wait("scalar", tv)
            kb.do("scalar", lambda e: e.activation(st_ln[:], st_var[:], AF.Ln, bias=f32i(1e-5)))
            ta2 = kb.do("scalar", lambda e: e.activation(st_rstd[:], st_ln[:], AF.Exp, scale=f32i(-0.5)), "a")
            kb.wait("vector", ta2, t_xscr_free[0])
            return kb.do("vector", lambda e: e.tensor_scalar(
                xscr[:], xtile, st_mean[:], st_rstd[:], op0=ALU.subtract, op1=ALU.mult), "v")

        def ln_transposes(dest_fn, permuted, tapply):
            kb.wait("tensor", tapply)
            last = {}
            for kc in range(16):
                b = 6 + (kc & 1)
                kb.wait("tensor", bankfree[b])
                if permuted:
                    j, e2 = kc >> 1, kc & 1
                    src = xscr[:, 256 * j + e2: 256 * j + e2 + 255: 2]
                else:
                    src = xscr[:, kc * 128:(kc + 1) * 128]
                tp = kb.do("tensor", lambda e, s=src, bb=b: e.transpose(psb[bb][:, 0:128], s, ident[:]), "p")
                eng = "vector" if (kc & 1) == 0 else "scalar"
                sem = "v" if eng == "vector" else "a"
                kb.wait(eng, tp)
                dst = dest_fn(kc)
                if eng == "vector":
                    tcp = kb.do("vector", lambda e, d=dst, bb=b: e.tensor_copy(d, psb[bb][:, 0:128]), sem)
                else:
                    tcp = kb.do("scalar", lambda e, d=dst, bb=b: e.copy(d, psb[bb][:, 0:128]), sem)
                bankfree[b] = tcp
                last[kc] = tcp
            t_xscr_free[0] = kb.cur("p")
            return last

        tstrip = A2.reset().bf16(2048)
        t_strip_free = None
        kb.wait("vector", t_x)
        kb.wait("scalar", t_x)
        if _DBG:
            kb.wait("sync", t_x)
            kb.dma("sync", dbg_x0, tile_home(0), sem="wdbg")
        for tt in range(NT if _STAGE >= 2 else 0):
            tapply = ln_stats_apply(tile_home(tt), [t_x])
            if t_strip_free is not None:
                kb.wait("vector", t_strip_free)
                kb.wait("scalar", t_strip_free)
            toks = ln_transposes(lambda kc: tstrip[:, kc * 128:(kc + 1) * 128], False, tapply)
            kb.wait("sync", toks[14], toks[15], kb.cur("v"), kb.cur("a"))
            t_strip_free = kb.dma("sync",
                                  xlnsp[:, :, tt * 128:(tt + 1) * 128].rearrange("kc p c -> p kc c"),
                                  tstrip[:].rearrange("p (kc c) -> p kc c", kc=16), sem="wsp")
        t_ln1 = t_strip_free

        barrier()
        a = A1.reset()
        cosb = a.bf16(2048); sinb = a.bf16(2048)
        kr = a.bf16(2048); vtok = a.bf16(2048)
        q0r = a.bf16(2048); q1r = a.bf16(2048)
        xb = [a.bf16(2048), a.bf16(2048)]
        A2.reset()
        ctxb = [A2.bf16(2048) for _ in range(4)]
        wbuf = A2.bf16(4096)
        wsl = [a.bf16(2048), A2.bf16(2048)]
        selb = A2.bf16(2048)
        tmpb = wbuf[:, 0:1024]

        kb.dma("sync", cosb, cos_in, sem="wt2")
        t_tab = kb.dma("sync", sinb, sinf_in, sem="wt2")
        t_sel = kb.dma("sync", selb[0:16, :], selT_in, sem="wt2")

        t_xb_free = [None, None]
        t_slab_free = [None, None]

        def rope(src, dst, t_src_ready):
            kb.wait("vector", t_src_ready, t_tab)
            rot = wbuf[:, 0:2048]
            tb = wbuf[:, 2048:2560]
            kb.do("vector", lambda e: e.tensor_copy(rot[0:64, :], src[64:128, :]))
            kb.do("vector", lambda e: e.tensor_copy(rot[64:128, :], src[0:64, :]))
            for sp in range(4):
                sl = slice(sp * 512, (sp + 1) * 512)
                kb.do("vector", lambda e, s=sl: e.tensor_tensor(tb, src[:, s], cosb[:, s], op=ALU.mult))
                kb.do("vector", lambda e, s=sl: e.tensor_tensor(dst[:, s], rot[:, s], sinb[:, s], op=ALU.mult))
                tk = kb.do("vector", lambda e, s=sl: e.tensor_tensor(dst[:, s], dst[:, s], tb, op=ALU.add), "v")
            kb.wait("vector", tk)
            return tk

        def proj_wavelet(slab_aps, slab_toks, nblk, t_extra=None):
            for b in range(nblk):
                for sp in range(4):
                    kb.wait("tensor", bankfree[4 * b + sp])
            for kc in range(NKC):
                pb = kc & 1
                kb.wait("sync", t_xb_free[pb])
                tl = kb.dma("sync", xb[pb], xlnsp[kc], sem=("wb0" if pb == 0 else "wb1"))
                kb.wait("tensor", tl, t_extra, *slab_toks)
                for b in range(nblk):
                    for sp in range(4):
                        last = (b == nblk - 1 and sp == 3)
                        kb.do("tensor", lambda e, bb=b, ss=sp, k=kc, p=pb: e.matmul(
                            ps[4 * bb + ss][:], slab_aps[bb][:, k * 128:(k + 1) * 128],
                            xb[p][:, ss * 512:(ss + 1) * 512],
                            start=(k == 0), stop=(k == NKC - 1)),
                            "p" if (kc == NKC - 1 or last) else None)
                t_xb_free[pb] = kb.cur("p")
            return kb.cur("p")

        def evac_block(b0, dst, t_mm):
            kb.wait("scalar", t_mm)
            tk = None
            for sp in range(4):
                tk = kb.do("scalar", lambda e, s=sp: e.copy(dst[:, s * 512:(s + 1) * 512], ps[b0 + s][:]), "a")
            kb.wait("scalar", tk)
            for sp in range(4):
                bankfree[b0 + sp] = tk
            return tk

        nheads_done = 0
        for kv in range(_A_GROUPS if _STAGE >= 3 else 0):
            kb.wait("gpsimd", t_slab_free[0])
            tk_sl = kb.dma("gpsimd", wsl[0].rearrange("p (kc o) -> p kc o", kc=16), wkt[:, kv * 128:(kv + 1) * 128].rearrange("(kc p) o -> p kc o", p=128), sem="g0")
            kb.wait("gpsimd", t_slab_free[1])
            tv_sl = kb.dma("gpsimd", wsl[1].rearrange("p (kc o) -> p kc o", kc=16), wvt[:, kv * 128:(kv + 1) * 128].rearrange("(kc p) o -> p kc o", p=128), sem="g1")
            t_mm = proj_wavelet(wsl, [tk_sl, tv_sl], 2, t_extra=t_ln1)
            if _A_LVL < 2:
                continue
            t_kev = evac_block(0, sqscr[:], t_mm)
            if _DBG and kv == 0:
                kb.wait("sync", t_kev, t_ln1)
                kb.dma("sync", dbg_kp, sqscr[:], sem="wdbg")
                kb.dma("sync", dbg_x[0], xlnsp[0], sem="wdbg")
                tdbg = kb.dma("sync", dbg_x[1], xlnsp[15], sem="wdbg")
                for e_ in ("tensor", "vector", "scalar", "gpsimd"):
                    kb.wait(e_, tdbg)
            if _DBG and kv == 0 and _A_LVL == 2:
                kb.wait("sync", t_mm, kb.cur("g0"), kb.cur("g1"))
                kb.dma("sync", dbg_ts, wsl[0], sem="wdbg")
                tdbg = kb.dma("sync", dbg_xs, xb[1], sem="wdbg")
                for e_ in ("tensor", "vector", "scalar", "gpsimd"):
                    kb.wait(e_, tdbg)
            t_kr = rope(sqscr[:], kr, t_kev) if _A_LVL >= 3 else t_kev
            if _A_LVL < 4:
                continue
            t_vev = evac_block(4, q0r, t_mm)
            kb.wait("tensor", t_vev)
            for tkc in range(NKC):
                b = 6 + (tkc & 1)
                kb.wait("tensor", bankfree[b])
                tp = kb.do("tensor", lambda e, k=tkc, bb=b: e.transpose(psb[bb][:, 0:128], q0r[:, k * 128:(k + 1) * 128], ident[:]), "p")
                kb.wait("vector", tp)
                tcp = kb.do("vector", lambda e, k=tkc, bb=b: e.tensor_copy(vtok[:, k * 128:(k + 1) * 128], psb[bb][:, 0:128]), "v")
                bankfree[b] = tcp
            t_vtok = kb.cur("v")
            kb.wait("vector", t_vtok)
            if _A_LVL < 5:
                continue
            kb.wait("gpsimd", kb.cur("p"))
            tq0 = kb.dma("gpsimd", wsl[0].rearrange("p (kc o) -> p kc o", kc=16), wqt[:, (2 * kv) * 128:(2 * kv + 1) * 128].rearrange("(kc p) o -> p kc o", p=128), sem="g0")
            tq1 = kb.dma("gpsimd", wsl[1].rearrange("p (kc o) -> p kc o", kc=16), wqt[:, (2 * kv + 1) * 128:(2 * kv + 2) * 128].rearrange("(kc p) o -> p kc o", p=128), sem="g1")
            t_mmq = proj_wavelet(wsl, [tq0, tq1], 2)
            t_slab_free = [t_mmq, t_mmq]
            kb.wait("scalar", t_kr)
            tq0e = evac_block(0, sqscr[:], t_mmq)
            t_q0r = rope(sqscr[:], q0r, tq0e)
            kb.wait("scalar", t_q0r)
            tq1e = evac_block(4, xscr[:], t_mmq)
            t_q1r = rope(xscr[:], q1r, tq1e)
            t_xscr_free[0] = kb.cur("v")

            for g in range(2 if _A_ATTN else 0):
                pass
            for g in range(2 if _A_ATTN else 0):
                qh = q0r if g == 0 else q1r
                t_qh = t_q0r if g == 0 else t_q1r
                slot = ctxb[nheads_done % 4]
                for tqt in range(NT):
                    tmm = None
                    for j in range(4):
                        kb.wait("tensor", bankfree[j], t_kr, t_qh)
                        for h9 in (0, 1):
                            tmm = kb.do("tensor", lambda e, q=tqt, jj=j, h8=h9, qv=qh: e.matmul(
                                ps[jj][:, h8 * 256:(h8 + 1) * 256], qv[:, q * 128:(q + 1) * 128],
                                kr[:, jj * 512 + h8 * 256: jj * 512 + (h8 + 1) * 256],
                                start=True, stop=True), "p")
                    kb.wait("vector", tmm)
                    if _DBG and kv == 0 and g == 0 and tqt == 0:
                        kb.do("vector", lambda e: e.tensor_copy(xscr[:, 0:256], ps[0][:, 0:256]), "v")
                        kb.do("vector", lambda e: e.tensor_copy(xscr[:, 256:512], ps[0][:, 256:512]), "v")
                        kb.do("vector", lambda e, qv=qh: e.tensor_copy(xscr[:, 512:640], qv[:, 0:128]), "v")
                        tpd = kb.do("vector", lambda e: e.tensor_copy(xscr[:, 640:1152], kr[:, 0:512]), "v")
                        kb.wait("vector", tpd)
                        kb.wait("sync", tpd)
                        tdbg = kb.dma("sync", dbg_ts[:, 512:1664], xscr[:, 0:1152], sem="wdbg")
                        for e_ in ("tensor", "vector", "scalar", "gpsimd"):
                            kb.wait(e_, tdbg)
                    tred = None
                    for j in range(4):
                        tred = kb.do("vector", lambda e, jj=j: e.tensor_reduce(
                            mred[:, jj:jj + 1], ps[jj][:], AX.X, ALU.max), "v")
                        bankfree[j] = tred
                    kb.wait("vector", tred)
                    t9 = kb.do("vector", lambda e, q=tqt: e.tensor_reduce(
                        negmc[:, q:q + 1], mred[:], AX.X, ALU.max), "v")
                    kb.wait("vector", t9)
                    t9b = kb.do("vector", lambda e, q=tqt: e.tensor_scalar(
                        negmc[:, q:q + 1], negmc[:, q:q + 1], -1.0, None, op0=ALU.mult), "v")
                    if _DBG and kv == 0 and g == 0 and tqt == 0:
                        kb.wait("sync", t9b)
                        kb.dma("sync", dbg_st[:, 0:1], negmc[:, 0:1], sem="wdbg",
                               allow_slow_non_contiguous=True)
                        tdbg = kb.dma("sync", dbg_st[:, 4:8], mred[:], sem="wdbg",
                                      allow_slow_non_contiguous=True)
                        for e_ in ("tensor", "vector", "scalar", "gpsimd"):
                            kb.wait(e_, tdbg)
                t_negm = kb.cur("v")
                kb.wait("vector", t_negm)
                kb.wait("tensor", bankfree[4], t_negm)
                tp = kb.do("tensor", lambda e: e.matmul(
                    ps[4][0:16, 0:128], negmc[:], identf[:],
                    start=True, stop=True), "p")
                kb.wait("vector", tp)
                t_mrow = kb.do("vector", lambda e: e.tensor_copy(mT16[:], ps[4][0:16, 0:128]), "v")
                kb.wait("vector", t_mrow)
                bankfree[4] = t_mrow
                kb.wait("tensor", t_mrow, t_sel)
                if _DBG and kv == 0 and g == 0:
                    kb.wait("sync", t_negm, t_mrow)
                    tdbg = kb.dma("sync", dbg_ts[0:16, 128:256], mT16[:], sem="wdbg")
                    for e_ in ("tensor", "vector", "scalar", "gpsimd"):
                        kb.wait(e_, tdbg)
                for tqc in range(8):
                    qsl = slice(tqc * 256, (tqc + 1) * 256)
                    for tkc in range(NKC):
                        b = tkc & 1
                        kb.wait("tensor", bankfree[b], t_kr, t_qh)
                        kb.do("tensor", lambda e, k=tkc, bb=b, qv=qh, q=qsl: e.matmul(
                            ps[bb][:, 0:256], kr[:, k * 128:(k + 1) * 128], qv[:, q],
                            start=True, stop=False))
                        tp = None
                        for hf in (0, 1):
                            t5 = 2 * tqc + hf
                            tp = kb.do("tensor", lambda e, bb=b, h2=hf, q5=t5: e.matmul(
                                ps[bb][:, h2 * 128:(h2 + 1) * 128],
                                selb[0:16, q5 * 128:(q5 + 1) * 128], mT16[:],
                                start=False, stop=True), "p")
                        kb.wait("scalar", tp)
                        te = kb.do("scalar", lambda e, k=tkc, bb=b: e.activation(
                            wbuf[:, k * 256:(k + 1) * 256], ps[bb][:, 0:256], AF.Exp,
                            scale=f32i(INV_SQRT_D)), "a")
                        bankfree[b] = te
                    t_exp = kb.cur("a")
                    kb.wait("scalar", t_exp)
                    kb.wait("tensor", t_exp, bankfree[2], bankfree[3])
                    for tkc in range(NKC):
                        kb.do("tensor", lambda e, k=tkc: e.matmul(
                            ps[2][0:1, 0:256], onesbf[:], wbuf[:, k * 256:(k + 1) * 256],
                            start=(k == 0), stop=(k == NKC - 1)),
                            "p" if tkc == NKC - 1 else None)
                        kb.do("tensor", lambda e, k=tkc: e.matmul(
                            ps[3][:, 0:256], vtok[:, k * 128:(k + 1) * 128], wbuf[:, k * 256:(k + 1) * 256],
                            start=(k == 0), stop=(k == NKC - 1)),
                            "p" if tkc == NKC - 1 else None)
                    t_den = kb.cur("p")
                    kb.wait("vector", t_den)
                    tr_ = kb.do("vector", lambda e: e.reciprocal(rdenf[0:1, :], ps[2][0:1, 0:256]), "v")
                    kb.wait("vector", tr_)
                    bankfree[2] = tr_
                    kb.wait("tensor", tr_, bankfree[4])
                    tb_ = kb.do("tensor", lambda e: e.matmul(ps[4][:, 0:256], onesf1[:], rdenf[0:1, :],
                                                             start=True, stop=True), "p")
                    kb.wait("vector", tb_)
                    tc1 = kb.do("vector", lambda e: e.tensor_copy(bcs[:], ps[4][:, 0:256]), "v")
                    bankfree[4] = tc1
                    kb.wait("vector", tc1)
                    tc2 = kb.do("vector", lambda e, q=qsl, s2=slot: e.tensor_tensor(
                        s2[:, q], ps[3][:, 0:256], bcs[:], op=ALU.mult), "v")
                    kb.wait("vector", tc2)
                    bankfree[3] = tc2
                    if _DBG and kv == 0 and g == 0 and tqc == 0:
                        kb.wait("tensor", bankfree[5])
                        tlm = kb.do("tensor", lambda e, qv=qh: e.matmul(
                            ps[5][:, 0:256], qv[:, 0:128], kr[:, 0:256],
                            start=True, stop=True), "p")
                        kb.wait("scalar", tlm)
                        kb.do("scalar", lambda e: e.copy(xscr[:, 1280:1536], ps[5][:, 0:256]), "a")
                        tsc = kb.do("scalar", lambda e, qv=qh: e.copy(xscr[:, 1536:1664], qv[:, 0:128]), "a")
                        bankfree[5] = tsc
                        kb.wait("sync", tsc, t_exp, tc1, tc2, t_qh, t_kr)
                        kb.dma("sync", dbg_ts[:, 1664:1920], xscr[:, 1280:1536], sem="wdbg")
                        kb.dma("sync", dbg_xs[:, 0:128], xscr[:, 1536:1664], sem="wdbg")
                        kb.dma("sync", dbg_w, wbuf, sem="wdbg")
                        kb.dma("sync", dbg_q, qh, sem="wdbg")
                        tdbg = kb.dma("sync", dbg_k, kr, sem="wdbg")
                        for e_ in ("tensor", "vector", "scalar", "gpsimd"):
                            kb.wait(e_, tdbg)
                if _DBG and kv == 0 and g == 0:
                    kb.wait("sync", kb.cur("v"))
                    kb.dma("sync", dbg_c, slot, sem="wdbg")
                nheads_done += 1

            if _A_ATTN and _A_OUT and nheads_done % 4 == 0:
                t_ctx = kb.cur("v")
                oslab = [wsl[0], wsl[1], xb[0], xb[1]]
                otoks = []
                kb.wait("gpsimd", kb.cur("p"))
                for hh in range(4):
                    h = nheads_done - 4 + hh
                    otoks.append(kb.dma("gpsimd", oslab[hh],
                                        wot[h * 128:(h + 1) * 128, :], sem="go"))
                for tt in range(NT):
                    for fc in range(4):
                        b = (tt * 4 + fc) & 1
                        kb.wait("tensor", bankfree[b], t_ctx, *otoks)
                        for hh in range(4):
                            kb.do("tensor", lambda e, h2=hh, t2=tt, f2=fc, bb=b, o4=oslab: e.matmul(
                                ps[bb][:], ctxb[h2][:, t2 * 128:(t2 + 1) * 128],
                                o4[h2][:, f2 * 512:(f2 + 1) * 512],
                                start=(h2 == 0), stop=(h2 == 3)),
                                "p" if hh == 3 else None)
                        tpm = kb.cur("p")
                        kb.wait("vector", tpm)
                        dstx = XR[:, tt * D + fc * 512: tt * D + (fc + 1) * 512]
                        tf = kb.do("vector", lambda e, d=dstx, bb=b: e.tensor_tensor(d, d, ps[bb][:], op=ALU.add), "v")
                        bankfree[b] = tf
                t_xb_free = [kb.cur("p"), kb.cur("p")]
                t_slab_free = [kb.cur("p"), kb.cur("p")]
        t_attn_a = kb.cur("v")

        def ffn_call():
            for half in (0, 1):
                lo = half * 8
                other = (1 - half) * 8
                barrier()
                for i in range(8):
                    tt = other + i
                    kb.dma("sync", xsp[tt], tile_home(tt), sem="wsp")
                t_spill = kb.cur("wsp")
                A2.reset()
                xlnP = A2.bf16(16384)
                for i in range(8):
                    tt = lo + i
                    tapply = ln_stats_apply(tile_home(tt), [])
                    ln_transposes(lambda kc, i2=i: xlnP[:, kc * 1024 + i2 * 128: kc * 1024 + (i2 + 1) * 128],
                                  True, tapply)
                t_xlnp = (kb.cur("v"), kb.cur("a"))
                A1.reset()
                inter = A1.bf16(16384)
                wa = Arena(XR[:, other * D:(other + 8) * D].bitcast(BF16), 32768)
                gwin = [wa.bf16(4096), wa.bf16(4096)]
                uwin = [wa.bf16(4096), wa.bf16(4096)]
                dtl = wa.bf16(8192)
                ge = [wa.f32(512), wa.f32(512)]
                t_win_free = [None, None]
                t_ge_free = [None, None]
                t_dtl_free = None
                for G in range(8):
                    t_inter_v = None
                    for kp in range(8):
                        pb = kp & 1
                        gw, uw = gwin[pb], uwin[pb]
                        w0 = G * 2048 + kp * 256
                        kb.wait("sync", t_spill, t_win_free[pb], t_gather)
                        tg = kb.dma("sync", gw.rearrange("p (a n) -> p a n", a=16), Gg[:, w0:w0 + 256].rearrange("(a p) n -> p a n", p=128), sem=("wg0" if pb == 0 else "wg1"))
                        tu = kb.dma("sync", uw.rearrange("p (a n) -> p a n", a=16), Gu[:, w0:w0 + 256].rearrange("(a p) n -> p a n", p=128), sem=("wg0" if pb == 0 else "wg1"))
                        for e2 in (0, 1):
                            cidx = kp * 2 + e2
                            for sp2 in (0, 1):
                                bg, bu = 2 * sp2, 2 * sp2 + 1
                                kb.wait("tensor", bankfree[bg], bankfree[bu], tg, tu, *t_xlnp)
                                for a3 in range(16):
                                    kb.do("tensor", lambda e, a4=a3, e4=e2, s4=sp2, g4=gw, b4=bg: e.matmul(
                                        ps[b4][:], g4[:, a4 * 256 + e4: a4 * 256 + e4 + 255: 2],
                                        xlnP[:, a4 * 1024 + s4 * 512: a4 * 1024 + (s4 + 1) * 512],
                                        start=(a4 == 0), stop=(a4 == 15)),
                                        "p" if a3 == 15 else None)
                                    kb.do("tensor", lambda e, a4=a3, e4=e2, s4=sp2, u4=uw, b4=bu: e.matmul(
                                        ps[b4][:], u4[:, a4 * 256 + e4: a4 * 256 + e4 + 255: 2],
                                        xlnP[:, a4 * 1024 + s4 * 512: a4 * 1024 + (s4 + 1) * 512],
                                        start=(a4 == 0), stop=(a4 == 15)),
                                        "p" if a3 == 15 else None)
                                tpm = kb.cur("p")
                                if e2 == 1 and sp2 == 1:
                                    t_win_free[pb] = tpm
                                kb.wait("scalar", tpm, t_ge_free[sp2])
                                tge = kb.do("scalar", lambda e, s4=sp2, b4=bg, g5=ge: e.activation(
                                    g5[s4][:, :], ps[b4][:], AF.Gelu_apprx_tanh), "a")
                                bankfree[bg] = tge
                                kb.wait("vector", tge)
                                tst = kb.do("vector", lambda e, c4=cidx, s4=sp2, b4=bu, g5=ge, i5=inter: e.tensor_tensor(
                                    i5[:, c4 * 1024 + s4 * 512: c4 * 1024 + (s4 + 1) * 512],
                                    ps[b4][:], g5[s4][:, :], op=ALU.mult), "v")
                                bankfree[bu] = tst
                                t_ge_free[sp2] = tst
                    t_inter_v = kb.cur("v")
                    for fc in range(4):
                        kb.wait("sync", t_dtl_free)
                        dts = []
                        for c in range(16):
                            kp2, e2 = c >> 1, c & 1
                            row0 = G * 2048 + e2 * 1024 + kp2 * 128
                            dts.append(kb.dma("sync", dtl[:, c * 512:(c + 1) * 512],
                                              Gd[row0:row0 + 128, fc * 512:(fc + 1) * 512], sem="wdt"))
                        kb.wait("tensor", t_inter_v, *dts)
                        kb.wait("tensor", *[bankfree[i] for i in range(8)])
                        for c in range(16):
                            for t8 in range(8):
                                kb.do("tensor", lambda e, c4=c, t4=t8, i5=inter, d5=dtl: e.matmul(
                                    ps[t4][:], i5[:, c4 * 1024 + t4 * 128: c4 * 1024 + (t4 + 1) * 128],
                                    d5[:, c4 * 512:(c4 + 1) * 512],
                                    start=(c4 == 0), stop=(c4 == 15)),
                                    "p" if c == 15 else None)
                        t_dtl_free = kb.cur("p")
                        kb.wait("vector", t_dtl_free)
                        for t8 in range(8):
                            tt = lo + t8
                            dstx = XR[:, tt * D + fc * 512: tt * D + (fc + 1) * 512]
                            tf = kb.do("vector", lambda e, d=dstx, b4=t8: e.tensor_tensor(
                                d, ps[b4][:], d, op=ALU.add), "v")
                            bankfree[t8] = tf
                barrier()
                for i in range(8):
                    tt = other + i
                    kb.dma("sync", tile_home(tt), xsp[tt], sem="wsp")
            barrier()

        def attn_b():
            barrier()
            A2.reset()
            A1.reset()
            x3a = A2.bf16(16384)
            x3b = A1.bf16(16384)

            def x3(kc):
                return (x3a[:, kc * 2048:(kc + 1) * 2048] if kc < 8
                        else x3b[:, (kc - 8) * 2048:(kc - 7) * 2048])

            for tt in range(NT):
                tapply = ln_stats_apply(tile_home(tt), [])
                ln_transposes(lambda kc, t2=tt: x3(kc)[:, t2 * 128:(t2 + 1) * 128], False, tapply)
            barrier()
            for tt in range(NT):
                kb.dma("sync", xsp[tt], tile_home(tt), sem="wsp")
            t_xspill = kb.cur("wsp")

            obuf = XR[:, 0:8 * D].bitcast(BF16)

            def oh(h):
                return obuf[:, h * 2048:(h + 1) * 2048]

            wk = Arena(XR[:, 8 * D:16 * D].bitcast(BF16), 32768)
            sqh = wk.bf16(2048)
            msb = wk.f32(2048)
            rsb = wk.f32(2048)
            msrow = msb[0:1, :]
            rstrow = rsb[0:1, :]
            scan_base = wk.off
            slab = wk.bf16(6144)
            qT = wk.bf16(2048)
            kT = wk.bf16(2048)
            vT = wk.bf16(2048)
            vtok2 = wk.bf16(2048)
            qdb = wk.bf16(256)
            kdb = wk.bf16(256)
            dscr = wk.f32(256)
            ddia = [wk.f32(256), wk.f32(256)]
            qdq = wk.bf16(256)
            kdk = wk.bf16(256)
            qkd = [wk.bf16(256), wk.bf16(256)]
            kdkT = [wk.bf16(128), wk.bf16(128)]
            kvst = wk.f32(128)
            kvb = wk.bf16(128)

            kb.wait("vector", t_xspill)
            kb.wait("scalar", t_xspill)
            kb.wait("gpsimd", t_xspill)
            t_slabf = None
            for h in range(16):
                kb.wait("gpsimd", t_slabf)
                tsl = kb.dma("gpsimd", slab.rearrange("p (kc o) -> p kc o", kc=16),
                             wqkvt[:, h * 384:(h + 1) * 384].rearrange("(kc p) o -> p kc o", p=128), sem="g0")
                for grpi, blks in ((0, (0, 1)), (1, (2,))):
                    for bi, b3 in enumerate(blks):
                        for sp in range(4):
                            kb.wait("tensor", bankfree[4 * bi + sp])
                    kb.wait("tensor", tsl)
                    for kc in range(NKC):
                        for bi, b3 in enumerate(blks):
                            for sp in range(4):
                                kb.do("tensor", lambda e, b4=b3, i4=bi, k4=kc, s4=sp: e.matmul(
                                    ps[4 * i4 + s4][:], slab[:, k4 * 384 + b4 * 128: k4 * 384 + (b4 + 1) * 128],
                                    x3(k4)[:, s4 * 512:(s4 + 1) * 512],
                                    start=(k4 == 0), stop=(k4 == NKC - 1)),
                                    "p" if kc == NKC - 1 else None)
                    tpm = kb.cur("p")
                    if grpi == 1:
                        t_slabf = tpm
                    kb.wait("scalar", tpm)
                    for bi, b3 in enumerate(blks):
                        dst = (qT, kT, vT)[b3]
                        tk = None
                        for sp in range(4):
                            tk = kb.do("scalar", lambda e, d=dst, i4=bi, s4=sp: e.activation(
                                d[:, s4 * 512:(s4 + 1) * 512], ps[4 * i4 + s4][:], AF.Silu), "a")
                        for sp in range(4):
                            bankfree[4 * bi + sp] = tk
                t_qkv = kb.cur("a")
                kb.wait("tensor", t_qkv)
                for tkc in range(NKC):
                    b = 6 + (tkc & 1)
                    kb.wait("tensor", bankfree[b])
                    tp = kb.do("tensor", lambda e, k=tkc, bb=b: e.transpose(
                        psb[bb][:, 0:128], vT[:, k * 128:(k + 1) * 128], ident[:]), "p")
                    kb.wait("vector", tp)
                    tcp = kb.do("vector", lambda e, k=tkc, bb=b: e.tensor_copy(
                        vtok2[:, k * 128:(k + 1) * 128], psb[bb][:, 0:128]), "v")
                    bankfree[b] = tcp
                tv1 = kb.do("vector", lambda e, h4=h: e.tensor_scalar(dscr, arrq[:], snegb[:, h4:h4 + 1], None, op0=ALU.mult), "v")
                kb.wait("scalar", tv1)
                ta1 = kb.do("scalar", lambda e: e.activation(qdb, dscr, AF.Exp), "a")
                kb.wait("vector", ta1)
                tv2 = kb.do("vector", lambda e, h4=h: e.tensor_scalar(dscr, arrk[:], snegb[:, h4:h4 + 1], None, op0=ALU.mult), "v")
                kb.wait("scalar", tv2)
                ta2 = kb.do("scalar", lambda e: e.activation(kdb, dscr, AF.Exp), "a")
                for ncb in range(2):
                    idx = idx0 if ncb == 0 else idx1
                    kb.wait("vector", ta2 if ncb == 0 else kb.cur("a"))
                    tv3 = kb.do("vector", lambda e, ix=idx, h4=h: e.tensor_scalar(dscr, ix[:], snegb[:, h4:h4 + 1], None, op0=ALU.mult), "v")
                    kb.wait("scalar", tv3)
                    kb.do("scalar", lambda e, n4=ncb: e.activation(ddia[n4], dscr, AF.Exp), "a")
                t_dec = kb.cur("a")
                tkv0 = kb.do("vector", lambda e: e.memset(kvst, 0.0), "v")
                t_kvud = tkv0
                t_blk_pe = None
                for b5 in range(8):
                    m0 = b5 * 256
                    msl = slice(m0, m0 + 256)
                    kb.wait("vector", t_dec, kb.cur("a"), t_blk_pe)
                    kb.do("vector", lambda e, s=msl: e.tensor_tensor(qdq, qT[:, s], qdb, op=ALU.mult))
                    kb.do("vector", lambda e, s=msl: e.tensor_tensor(kdk, kT[:, s], kdb, op=ALU.mult))
                    kb.wait("vector", t_kvud)
                    tkvb = kb.do("vector", lambda e: e.tensor_copy(kvb, kvst), "v")
                    kb.wait("vector", tkvb)
                    for ncb in range(2):
                        kb.wait("tensor", bankfree[4 + ncb], tkvb)
                        tqk = kb.do("tensor", lambda e, n4=ncb, s=msl, m5=m0: e.matmul(
                            ps[4 + n4][:, 0:256], kT[:, m5 + n4 * 128: m5 + (n4 + 1) * 128], qT[:, s],
                            start=True, stop=True), "p")
                        kb.wait("vector", tqk)
                        tqkd = kb.do("vector", lambda e, n4=ncb: e.tensor_tensor(
                            qkd[n4], ps[4 + n4][:, 0:256], ddia[n4], op=ALU.mult), "v")
                        bankfree[4 + ncb] = tqkd
                    kb.wait("tensor", bankfree[0], kb.cur("v"))
                    kb.do("tensor", lambda e: e.matmul(ps[0][:, 0:256], kvb, qdq, start=True, stop=False))
                    for ncb in range(2):
                        kb.do("tensor", lambda e, n4=ncb, b6=b5: e.matmul(
                            ps[0][:, 0:256], vtok2[:, (2 * b6 + n4) * 128:(2 * b6 + n4 + 1) * 128],
                            qkd[n4], start=False, stop=(n4 == 1)),
                            "p" if ncb == 1 else None)
                    tom = kb.cur("p")
                    kb.wait("scalar", tom)
                    toe = kb.do("scalar", lambda e, h4=h, s=msl: e.copy(oh(h4)[:, s], ps[0][:, 0:256]), "a")
                    bankfree[0] = toe
                    for ncb in range(2):
                        b = 6 + ncb
                        kb.wait("tensor", bankfree[b])
                        tp = kb.do("tensor", lambda e, n4=ncb, bb=b: e.transpose(
                            psb[bb][:, 0:128], kdk[:, n4 * 128:(n4 + 1) * 128], ident[:]), "p")
                        kb.wait("vector", tp)
                        tcp = kb.do("vector", lambda e, n4=ncb, bb=b: e.tensor_copy(kdkT[n4], psb[bb][:, 0:128]), "v")
                        bankfree[b] = tcp
                    kb.wait("tensor", bankfree[1], kb.cur("v"))
                    for ncb in range(2):
                        kb.do("tensor", lambda e, n4=ncb, b6=b5: e.matmul(
                            ps[1][:, 0:128], kdkT[n4], vtok2[:, (2 * b6 + n4) * 128:(2 * b6 + n4 + 1) * 128],
                            start=(n4 == 0), stop=(n4 == 1)),
                            "p" if ncb == 1 else None)
                    tdm = kb.cur("p")
                    kb.wait("vector", tdm)
                    tku = kb.do("vector", lambda e, h4=h: e.scalar_tensor_tensor(
                        kvst, kvst, bdb[:, h4:h4 + 1], ps[1][:, 0:128], op0=ALU.mult, op1=ALU.add), "v")
                    bankfree[1] = tku
                    t_kvud = tku
                    t_blk_pe = kb.cur("p")
            barrier()
            for sp in range(4):
                kb.wait("tensor", bankfree[sp])
            t_sqfree = None
            for h in range(16):
                kb.wait("scalar", t_sqfree)
                th = kb.do("scalar", lambda e, h4=h: e.activation(sqh, oh(h4), AF.Square), "a")
                kb.wait("tensor", th)
                for sp in range(4):
                    kb.do("tensor", lambda e, s4=sp, h4=h: e.matmul(
                        ps[s4][0:1, :], onesbf[:], sqh[:, s4 * 512:(s4 + 1) * 512],
                        start=(h4 == 0), stop=(h4 == 15)),
                        "p" if sp == 3 else None)
                t_sqfree = kb.cur("p")
            tsq = kb.cur("p")
            kb.wait("vector", tsq)
            tcm = None
            for sp in range(4):
                tcm = kb.do("vector", lambda e, s4=sp: e.tensor_copy(msrow[:, s4 * 512:(s4 + 1) * 512], ps[s4][0:1, :]), "v")
            kb.wait("vector", tcm)
            tms = kb.do("vector", lambda e: e.tensor_scalar(msrow[:], msrow[:], 1.0 / D, None, op0=ALU.mult), "v")
            for sp in range(4):
                bankfree[sp] = tms
            kb.wait("scalar", tms)
            kb.do("scalar", lambda e: e.activation(msrow[:], msrow[:], AF.Ln, bias=f32i(1e-6)))
            trs = kb.do("scalar", lambda e: e.activation(rstrow[:], msrow[:], AF.Exp, scale=f32i(-0.5)), "a")
            wk2 = Arena(XR[:, 8 * D:16 * D].bitcast(BF16), 32768).reset(scan_base)
            rstb = wk2.bf16(2048)
            kb.wait("tensor", trs, bankfree[4], bankfree[5], bankfree[6], bankfree[7])
            for sp in range(4):
                kb.do("tensor", lambda e, s4=sp: e.matmul(ps[4 + s4][:], onesf1[:], rstrow[:, s4 * 512:(s4 + 1) * 512],
                                                          start=True, stop=True), "p")
            trb = kb.cur("p")
            kb.wait("vector", trb)
            for sp in range(4):
                tcb = kb.do("vector", lambda e, s4=sp: e.tensor_copy(rstb[:, s4 * 512:(s4 + 1) * 512], ps[4 + s4][:]), "v")
                bankfree[4 + sp] = tcb
            t_rstb = kb.cur("v")
            t_gslabf = None
            for fc in range(16):
                kb.wait("gpsimd", t_gslabf)
                tgs = kb.dma("gpsimd", slab[:, 0:2048].rearrange("p (kc o) -> p kc o", kc=16),
                             wgatet[:, fc * 128:(fc + 1) * 128].rearrange("(kc p) o -> p kc o", p=128), sem="g1")
                for sp in range(4):
                    kb.wait("tensor", bankfree[sp], tgs, t_rstb)
                for kc in range(NKC):
                    for sp in range(4):
                        kb.do("tensor", lambda e, k4=kc, s4=sp: e.matmul(
                            ps[s4][:], slab[:, 0:2048][:, k4 * 128:(k4 + 1) * 128],
                            x3(k4)[:, s4 * 512:(s4 + 1) * 512],
                            start=(k4 == 0), stop=(k4 == NKC - 1)),
                            "p" if kc == NKC - 1 else None)
                t_gslabf = kb.cur("p")
                kb.wait("scalar", t_gslabf)
                tsg = None
                for sp in range(4):
                    tsg = kb.do("scalar", lambda e, s4=sp: e.activation(
                        sqh[:, s4 * 512:(s4 + 1) * 512], ps[s4][:], AF.Sigmoid), "a")
                kb.wait("vector", tsg)
                kb.do("vector", lambda e, f4=fc: e.tensor_tensor(oh(f4), oh(f4), rstb, op=ALU.mult))
                kb.do("vector", lambda e, f4=fc: e.tensor_scalar(oh(f4), oh(f4), nw[:, f4:f4 + 1], None, op0=ALU.mult))
                tgo = kb.do("vector", lambda e, f4=fc: e.tensor_tensor(oh(f4), oh(f4), sqh[:], op=ALU.mult), "v")
                for sp in range(4):
                    bankfree[sp] = tgo
            t_go = kb.cur("v")
            wotl = wk2.bf16(8192)
            stg = [wk2.f32(512), wk2.f32(512)]
            t_wo_free = None
            t_stg_free = [None, None]
            for fc in range(4):
                kb.wait("gpsimd", t_wo_free)
                wts = []
                for kc in range(16):
                    wts.append(kb.dma("gpsimd", wotl[:, kc * 512:(kc + 1) * 512],
                                      woutt[kc * 128:(kc + 1) * 128, fc * 512:(fc + 1) * 512], sem="gw2"))
                for tt in range(NT):
                    sb5 = tt & 1
                    kb.wait("sync", t_stg_free[sb5])
                    tload = kb.dma("sync", stg[sb5], xsp[tt][:, fc * 512:(fc + 1) * 512], sem=("ws0" if sb5 == 0 else "ws1"))
                    b = 2 + (tt & 1)
                    kb.wait("tensor", bankfree[b], t_go, *wts)
                    for kc in range(16):
                        kb.do("tensor", lambda e, k4=kc, t4=tt, b4=b: e.matmul(
                            ps[b4][:], obuf[:, k4 * 2048 + t4 * 128: k4 * 2048 + (t4 + 1) * 128],
                            wotl[:, k4 * 512:(k4 + 1) * 512],
                            start=(k4 == 0), stop=(k4 == 15)),
                            "p" if kc == 15 else None)
                    tpm = kb.cur("p")
                    kb.wait("vector", tpm, tload)
                    tfl = kb.do("vector", lambda e, s5=sb5, b4=b: e.tensor_tensor(
                        stg[s5], stg[s5], ps[b4][:], op=ALU.add), "v")
                    bankfree[b] = tfl
                    kb.wait("sync", tfl)
                    t_stg_free[sb5] = kb.dma("sync", xsp[tt][:, fc * 512:(fc + 1) * 512], stg[sb5], sem=("ws2" if sb5 == 0 else "ws3"))
                t_wo_free = kb.cur("p")
            barrier()
            kb.dma("sync", XR.rearrange("p (tt f) -> p tt f", tt=NT), xsp.rearrange("tt p f -> p tt f"), sem="wsp")
            barrier()

        barrier()
        if _STAGE >= 4:
            ffn_call()
        if _STAGE >= 5:
            attn_b()
        if _STAGE >= 6:
            ffn_call()
        barrier()
        t_y = kb.dma("sync", y_out.rearrange("(tt p) f -> p tt f", p=128), XR.rearrange("p (tt f) -> p tt f", tt=NT), sem="wy")
        kb.wait("sync", t_y)

        @block.sync
        def _(e):
            for fn in kb.q["sync"]:
                fn(e)

        @block.tensor
        def _(e):
            for fn in kb.q["tensor"]:
                fn(e)

        @block.vector
        def _(e):
            for fn in kb.q["vector"]:
                fn(e)

        @block.scalar
        def _(e):
            for fn in kb.q["scalar"]:
                fn(e)

        @block.gpsimd
        def _(e):
            for fn in kb.q["gpsimd"]:
                fn(e)

    return nc


def kernel(**inputs):
    prep = _host_prep(inputs)
    nc = _build()
    in_maps = []
    for c in range(NCORES):
        m = dict(prep["shared"])
        m.update(prep["per_core"][c])
        in_maps.append(m)
    res = run_bass_kernel_spmd(nc, in_maps, core_ids=list(range(NCORES)))
    if _DBG:
        import pickle
        dbg = {k: np.asarray(v) for k, v in res.results[0].items() if k != "y"}
        with open("/root/problem/dbg.pkl", "wb") as f:
            pickle.dump(dbg, f)
    out = np.stack([np.asarray(res.results[c]["y"], dtype=np.float32) for c in range(NCORES)], axis=0)
    return out
